# version 39 (final)
# speedup vs baseline: 1.1694x; 1.0096x over previous
import contextlib
import numpy as np
import ml_dtypes
import concourse.bass as bass
import concourse.mybir as mybir
from concourse.bass_utils import run_bass_kernel_spmd

F32 = mybir.dt.float32
BF16 = mybir.dt.bfloat16
I32 = mybir.dt.int32
AX = mybir.AxisListType
ALU = mybir.AluOpType
AF = mybir.ActivationFunctionType

B, S, D, H, E = 4, 4096, 2048, 16, 128
HID = 8192
HO = 8
OWNQ = 2048
DC = D // 128
NKT = S // 128
NB = 8
INV = 1.0 / float(np.sqrt(128.0))
EPS1 = 1e-5
EPS2 = float(np.finfo(np.float32).eps)
NEG_BIG = -100000.0

QENG = "gpsimd"
PIPE_Z = True
LACC = True

FULL_PHASE = 5


def build(nc, phase=FULL_PHASE, debug=()):
    dbg = set(debug)
    ap_in = {}
    def xin(name, shape, dt):
        ap_in[name] = nc.dram_tensor(name, list(shape), dt, kind="ExternalInput").ap()
        return ap_in[name]

    x_bT   = xin("x_bT",  [D, S], F32)
    xqT    = xin("xqT",   [D, OWNQ], F32)
    wq8    = xin("wq8",   [HO, D, E], F32)
    wk8    = xin("wk8",   [HO, D, E], F32)
    wv8t   = xin("wv8t",  [D, HO * E], F32)
    bqt    = xin("bqt",   [E, HO], F32)
    bkt    = xin("bkt",   [E, HO], F32)
    bvrow  = xin("bvrow", [1, HO * E], BF16)
    wo     = xin("wo",    [H, E, D], F32)
    bot    = xin("bot",   [128, DC], F32)
    wfc_s  = xin("wfc_s", [HID // 8, D], F32)
    wpj_s  = xin("wpj_s", [D // 8, HID], F32)
    zoffv  = xin("zoffv", [1, 1], I32)
    identb = xin("identb", [128, 128], BF16)
    onesrow_f = xin("onesrow_f", [1, 128], F32)
    onesrow_b = xin("onesrow_b", [1, 128], BF16)
    onescol_f = xin("onescol_f", [128, 1], F32)
    onescol_b = xin("onescol_b", [128, 1], BF16)
    neginvrow = xin("neginvrow", [1, 128], BF16)
    masksb = xin("masksb", [128, 256], BF16)
    epsrow = xin("epsrow", [1, 2], F32)

    yT = nc.dram_tensor("yT", [D, OWNQ], F32, kind="ExternalOutput").ap()

    w1ti = nc.dram_tensor("w1ti", [8 * 128, DC * 128], BF16)
    w1tg = nc.dram_tensor("w1tg", [8 * 8 * 128, DC * 128], BF16)
    w2ti = nc.dram_tensor("w2ti", [2 * 128, 64 * 128], BF16)
    w2tg = nc.dram_tensor("w2tg", [8 * 2 * 128, 64 * 128], BF16)
    zin  = nc.dram_tensor("zin",  [2 * 4 * 256, 2048], BF16)
    ZG   = nc.dram_tensor("ZG",   [2 * 4 * 512, 2048], BF16)
    x1s  = nc.dram_tensor("x1s",  [D, OWNQ], F32)

    dbg_out = {}
    for name, shape, dt in [
        ("dbg_xnr", [128, DC * S], BF16),
        ("dbg_kt", [E, S], BF16), ("dbg_qt", [E, S], BF16), ("dbg_v", [128, NKT * 256], BF16),
        ("dbg_z", [2 * 4 * 256, 2048], BF16),
        ("dbg_x1", [D, OWNQ], F32), ("dbg_xn2", [128, DC * OWNQ], BF16),
        ("dbg_w1", [8 * 128, DC * 128], BF16), ("dbg_w2", [2 * 128, 64 * 128], BF16),
        ("dbg_w1g", [64 * 128, DC * 128], BF16),
        ("dbg_w1g2", [64 * 128, DC * 128], BF16),
        ("dbg_a", [128, 64 * 512], BF16),
        ("dbg_bc", [128, 512], F32),
    ]:
        if name in dbg:
            dbg_out[name] = nc.dram_tensor(name, shape, dt, kind="ExternalOutput").ap()

    es = contextlib.ExitStack()
    EC = es.enter_context

    sem_names = ["qS", "qG", "cc", "tT", "tV", "tA",
                 "w1a", "w1b", "w2a", "w2b", "x1l", "xq", "ztl", "wst",
                 "xsa", "xsb", "qld", "ccw", "zst", "tG", "bvl"]
    SEM = {n: EC(nc.semaphore(n)) for n in sem_names}
    CNT = {n: 0 for n in sem_names}

    script = []

    def emit(eng, fn, waits=(), inc=None, k=1):
        wt = tuple(waits)
        def g(e, _w=wt, _f=fn, _i=inc, _k=k):
            for s, v in _w:
                e.wait_ge(SEM[s], v)
            ins = _f(e)
            if _i is not None:
                ins.then_inc(SEM[_i], _k)
        script.append((eng, g))
        if inc is not None:
            CNT[inc] += k
        return CNT[inc] if inc is not None else None

    def emit_multi_dma(eng, fn, waits=(), n_dma=1, sem="qS"):
        wt = tuple(waits)
        def g(e, _w=wt, _f=fn):
            for s, v in _w:
                e.wait_ge(SEM[s], v)
            _f(e)
        script.append((eng, g))
        CNT[sem] += 16 * n_dma
        return CNT[sem]

    def wait_only(eng, waits):
        wt = tuple(waits)
        def g(e, _w=wt):
            for s, v in _w:
                e.wait_ge(SEM[s], v)
        script.append((eng, g))

    sb = {}
    def alloc(name, shape, dt):
        sb[name] = EC(nc.sbuf_tensor(name, list(shape), dt))
        return sb[name]

    alloc("c_ident", [128, 128], BF16)
    alloc("c_ones_r", [1, 128], F32)
    alloc("c_ones_cb", [128, 1], BF16)
    alloc("c_ones_cf", [128, 1], F32)
    alloc("c_neginv", [1, 128], BF16)
    alloc("c_ones_rb", [1, 128], BF16)
    alloc("c_masks", [128, 256], BF16)
    alloc("c_bqt", [E, HO], F32)
    alloc("c_bkt", [E, HO], F32)

    alloc("c_bot", [128, DC], F32)
    alloc("c_eps", [1, 2], F32)
    alloc("s_zoff", [1, 1], I32)

    alloc("r_s1", [1, 512], F32)
    alloc("r_ms", [1, 512], F32)
    alloc("r_rstd", [1, 512], F32)
    alloc("r_nmr", [1, 512], F32)
    alloc("bc_rstd", [128, 512], F32)
    alloc("bc_nmr", [128, 512], F32)
    alloc("t_sq0", [128, 512], BF16)
    alloc("t_sq1", [128, 512], BF16)
    alloc("t_n0", [128, 512], F32)

    esA = contextlib.ExitStack()
    def allocA(name, shape, dt):
        sb[name] = esA.enter_context(nc.sbuf_tensor(name, list(shape), dt))
        return sb[name]
    allocA("xnr", [128, DC * S], BF16)
    esQ = contextlib.ExitStack()
    if phase >= 2:
        specs = [("q_wb", [128, 1024], BF16), ("q_s", [128, 16], F32),
                 ("q_sb", [128, 16], BF16), ("q_sc", [128, 16], F32),
                 ("q_t", [128, 1024], BF16),
                 ("q_qs", [128, 1024], BF16), ("q_qsb", [128, 1024], BF16),
                 ("q_tr", [128, 128], BF16)]
        if phase == 2:
            specs.append(("q_stg", [128, 1024], F32))
        for nm, shp, dt in specs:
            sb[nm] = esQ.enter_context(nc.sbuf_tensor(nm, shp, dt))

    ps = {}
    def palloc(name, shape, dt):
        ps[name] = EC(nc.psum_tensor(name, list(shape), dt))
        return ps[name]
    palloc("P0", [128, 512], F32)
    palloc("P1", [128, 512], F32)
    palloc("P2", [128, 512], F32)
    palloc("P3", [128, 512], F32)
    palloc("PZ", [128, 512], F32)
    palloc("PL", [1, 512], F32)
    palloc("PTB", [128, 128], BF16)

    XNR = sb["xnr"]
    def xnr_sl(dc_, c0, n):
        return XNR[:, dc_ * S + c0: dc_ * S + c0 + n]

    for nm, src in [("c_ident", identb), ("c_ones_r", onesrow_f), ("c_ones_cb", onescol_b),
                    ("c_neginv", neginvrow), ("c_ones_rb", onesrow_b), ("c_ones_cf", onescol_f), ("c_masks", masksb), ("c_bqt", bqt),
                    ("c_bkt", bkt), ("c_bot", bot), ("c_eps", epsrow), ("s_zoff", zoffv)]:
        emit("sync", lambda e, _n=nm, _s=src: e.dma_start(out=sb[_n][:], in_=_s), inc="qS", k=16)
    consts_ld = CNT["qS"]

    esX = contextlib.ExitStack()
    sb["xstg0"] = esX.enter_context(nc.sbuf_tensor("xstg0", [128, S // 2], F32))
    sb["xstg1"] = esX.enter_context(nc.sbuf_tensor("xstg1", [128, S // 2], F32))
    xcasts = []
    for i in range(2 * DC):
        dc_, hf_ = i // 2, i % 2
        stg = "xstg0" if i % 2 == 0 else "xstg1"
        wt = [("tA", xcasts[i - 2])] if i >= 2 else []
        xsem = "xsa" if i % 2 == 0 else "xsb"
        xq_ = "gpsimd" if i % 2 == 0 else "sync"
        ldn = emit(xq_, lambda e, _dc=dc_, _h=hf_, _s=stg: e.dma_start(
            out=sb[_s][:], in_=x_bT[_dc * 128:(_dc + 1) * 128, _h * (S // 2):(_h + 1) * (S // 2)]),
            waits=wt, inc=xsem, k=16)
        xcasts.append(emit("scalar", lambda e, _dc=dc_, _h=hf_, _s=stg: e.activation(
            out=xnr_sl(_dc, _h * (S // 2), S // 2), in_=sb[_s][:], func=AF.Identity, bias=0.0, scale=1.0),
            waits=[(xsem, ldn)], inc="tA"))
    x_cast = CNT["tA"]

    _qb_idx = [0]
    def quant_block(w_src_ap, ncols, ngrp, dst_fn, n_cchunks, v_after=None):
        qs_name = "q_qs" if _qb_idx[0] % 2 == 0 else "q_qsb"
        _qb_idx[0] += 1
        """process one [128, ncols] row-block; transpose chunks to dst."""
        stg = "q_stg" if phase == 2 else "wstg"
        ld = emit("sync", lambda e, _s=stg: e.dma_start(out=sb[_s][:, 0:ncols], in_=w_src_ap),
                  waits=[("tA", CNT["tA"]), ("tG", CNT["tG"])], inc="qld", k=16)
        cst = emit("scalar", lambda e, _s=stg: e.activation(
            out=sb["q_wb"][:, 0:ncols], in_=sb[_s][:, 0:ncols], func=AF.Identity, bias=0.0, scale=1.0),
            waits=[("qld", ld), ("tG", CNT["tG"])], inc="tA")
        v0 = emit("vector", lambda e: e.tensor_reduce(
            out=sb["q_s"][:, 0:ngrp],
            in_=sb["q_wb"].ap()[:, 0:ncols].rearrange("p (g k) -> p g k", k=64),
            axis=AX.X, op=ALU.add, apply_absolute_value=True),
            waits=[("tA", cst)], inc="tV")
        v0 = emit("vector", lambda e: e.tensor_scalar(out=sb["q_sb"][:, 0:ngrp], in0=sb["q_s"][:, 0:ngrp],
                                                      scalar1=1.0 / 64, scalar2=None, op0=ALU.mult),
                  waits=[("tV", v0)], inc="tV")
        v0 = emit("vector", lambda e: e.tensor_scalar(out=sb["q_sc"][:, 0:ngrp], in0=sb["q_sb"][:, 0:ngrp],
                                                      scalar1=1e-8, scalar2=None, op0=ALU.max),
                  waits=[("tV", v0)], inc="tV")
        v0 = emit("vector", lambda e: e.reciprocal(out=sb["q_s"][:, 0:ngrp], in_=sb["q_sc"][:, 0:ngrp]),
                  waits=[("tV", v0)], inc="tV")
        def bcast_ap(t, ngrp_):
            a0 = t.ap()[:, 0:ngrp_]
            return bass.AP(tensor=a0.tensor, offset=a0.offset, ap=[a0.ap[0], a0.ap[1], [0, 64]])
        qsem = "tG" if QENG == "gpsimd" else "tV"
        g0 = emit(QENG, lambda e: e.tensor_tensor(
            out=sb["q_t"].ap()[:, 0:ncols].rearrange("p (g k) -> p g k", k=64),
            in0=sb["q_wb"].ap()[:, 0:ncols].rearrange("p (g k) -> p g k", k=64),
            in1=bcast_ap(sb["q_s"], ngrp), op=ALU.mult),
            waits=[("tV", v0)], inc=qsem)
        g0 = emit(QENG, lambda e: e.tensor_scalar(out=sb["q_wb"][:, 0:ncols], in0=sb["q_t"][:, 0:ncols],
                                                  scalar1=0.5, scalar2=None, op0=ALU.is_gt),
                  waits=[(qsem, g0)], inc=qsem)
        g0 = emit(QENG, lambda e, _q=qs_name: e.tensor_scalar(out=sb[_q][:, 0:ncols], in0=sb["q_t"][:, 0:ncols],
                                                  scalar1=-0.5, scalar2=None, op0=ALU.is_lt),
                  waits=[(qsem, g0)], inc=qsem)
        g0 = emit(QENG, lambda e, _q=qs_name: e.tensor_tensor(out=sb["q_t"][:, 0:ncols], in0=sb["q_wb"][:, 0:ncols],
                                                  in1=sb[_q][:, 0:ncols], op=ALU.subtract),
                  waits=[(qsem, g0)], inc=qsem)
        vq = emit(QENG, lambda e, _q=qs_name: e.tensor_tensor(
            out=sb[_q].ap()[:, 0:ncols].rearrange("p (g k) -> p g k", k=64),
            in0=sb["q_t"].ap()[:, 0:ncols].rearrange("p (g k) -> p g k", k=64),
            in1=bcast_ap(sb["q_sc"], ngrp), op=ALU.mult),
            waits=[(qsem, g0), ("tT", CNT["tT"])], inc=qsem)
        if v_after is not None:
            v_after()

        def do_transposes(_vq=vq, _dst=dst_fn, _n=n_cchunks, _qn=qs_name):
            for c in range(_n):
                t = emit("tensor", lambda e, _c=c: e.transpose(
                    ps["PTB"][:], sb[_qn][:, _c * 128:(_c + 1) * 128], sb["c_ident"][:]),
                    waits=[("tV", CNT["tV"]), (qsem, _vq)], inc="tT")
                v = emit("vector", lambda e: e.tensor_copy(sb["q_tr"][:], ps["PTB"][:]),
                         waits=[("tT", t), ("qS", CNT["qS"])], inc="tV")
                emit("sync", lambda e, _c=c: e.dma_start(
                    out=_dst(_c), in_=sb["q_tr"][:]), waits=[("tV", v)], inc="qS", k=16)
        return do_transposes

    quant_tasks = []
    if phase >= 2:
        for blk in range(8):
            for cc_ in range(2):
                quant_tasks.append((
                    wfc_s[blk * 128:(blk + 1) * 128, cc_ * 1024:(cc_ + 1) * 1024], 1024, 16,
                    (lambda _b, _c: lambda i: w1ti.ap()[_b * 128:(_b + 1) * 128,
                                                        (_c * 8 + i) * 128:(_c * 8 + i + 1) * 128])(blk, cc_),
                    8))
        for blk in range(2):
            for ccnk in range(8):
                quant_tasks.append((
                    wpj_s[blk * 128:(blk + 1) * 128, ccnk * 1024:(ccnk + 1) * 1024], 1024, 16,
                    (lambda _b, _cc: lambda i: w2ti.ap()[_b * 128:(_b + 1) * 128,
                                                         (_cc * 8 + i) * 128:(_cc * 8 + i + 1) * 128])(blk, ccnk),
                    8))

    def emit_gather(which):
        qn = CNT["qS"]
        if which == 1:
            emit("gpsimd", lambda e: e.collective_compute(
                "AllGather", ALU.bypass, replica_groups=[list(range(8))],
                ins=[w1ti.ap().opt()], outs=[w1tg.ap().opt()]),
                waits=[("qS", qn)], inc="ccw")
        else:
            emit("gpsimd", lambda e: e.collective_compute(
                "AllGather", ALU.bypass, replica_groups=[list(range(8))],
                ins=[w2ti.ap().opt()], outs=[w2tg.ap().opt()]),
                waits=[("qS", qn)], inc="ccw")

    cc_w = 2
    if phase == 2:
        for t in quant_tasks:
            quant_block(*t)()
        emit_gather(1)
        emit_gather(2)
        if "dbg_w1" in dbg_out:
            emit("sync", lambda e: e.dma_start(out=dbg_out["dbg_w1"], in_=w1ti.ap()),
                 waits=[("qS", CNT["qS"])], inc="qS", k=16)
        if "dbg_w2" in dbg_out:
            emit("sync", lambda e: e.dma_start(out=dbg_out["dbg_w2"], in_=w2ti.ap()),
                 waits=[("qS", CNT["qS"])], inc="qS", k=16)

    def ln_block(get_tile, set_tile, n_dc, eps_idx, wait_pe_extra=None, sq_wait=()):
        for dc_ in range(n_dc):
            emit("tensor", lambda e, _dc=dc_: e.matmul(
                ps["PL"][0:1, :], sb["c_ones_cb"][:], get_tile(_dc),
                start=(_dc == 0), stop=(_dc == n_dc - 1)),
                waits=((wait_pe_extra or []) + [("tV", CNT["tV"])]) if dc_ == 0 else [], inc="tT")
        s1_t = CNT["tT"]
        for dc_ in range(n_dc):
            buf = "t_sq0" if dc_ % 2 == 0 else "t_sq1"
            v = emit("vector", lambda e, _dc=dc_, _b=buf: e.tensor_tensor(
                out=sb[_b][:], in0=get_tile(_dc), in1=get_tile(_dc), op=ALU.mult),
                waits=(list(sq_wait) if dc_ == 0 else ([("tT", CNT["tT"])] if dc_ >= 2 else [])),
                inc="tV")
            emit("tensor", lambda e, _dc=dc_, _b=buf: e.matmul(
                ps["P2"][0:1, :], sb["c_ones_cb"][:], sb[_b][:],
                start=(_dc == 0), stop=(_dc == n_dc - 1)),
                waits=[("tV", v)], inc="tT")
        s2_t = CNT["tT"]
        v = emit("vector", lambda e: e.tensor_scalar(out=sb["r_s1"][:], in0=ps["PL"][0:1, :],
                                                     scalar1=1.0 / (n_dc * 128), scalar2=None, op0=ALU.mult),
                 waits=[("tT", s2_t)], inc="tV")
        v = emit("vector", lambda e: e.tensor_scalar(out=sb["r_ms"][:], in0=ps["P2"][0:1, :],
                                                     scalar1=1.0 / (n_dc * 128), scalar2=None, op0=ALU.mult),
                 waits=[("tV", v)], inc="tV")
        v = emit("vector", lambda e: e.tensor_tensor(out=sb["r_nmr"][:], in0=sb["r_s1"][:], in1=sb["r_s1"][:], op=ALU.mult),
                 waits=[("tV", v)], inc="tV")
        v_var = emit("vector", lambda e: e.tensor_tensor(out=sb["r_ms"][:], in0=sb["r_ms"][:], in1=sb["r_nmr"][:], op=ALU.subtract),
                     waits=[("tV", v)], inc="tV")
        a1 = emit("scalar", lambda e: e.activation(out=sb["r_nmr"][:], in_=sb["r_ms"][:], func=AF.Ln,
                                                   bias=sb["c_eps"][0:1, eps_idx:eps_idx + 1], scale=1.0),
                  waits=[("tV", v_var)], inc="tA")
        a2 = emit("scalar", lambda e: e.activation(out=sb["r_rstd"][:], in_=sb["r_nmr"][:], func=AF.Exp, bias=0.0, scale=-0.5),
                  waits=[("tA", a1)], inc="tA")
        v_n = emit("vector", lambda e: e.tensor_tensor(out=sb["r_nmr"][:], in0=sb["r_s1"][:], in1=sb["r_rstd"][:], op=ALU.mult),
                   waits=[("tA", a2)], inc="tV")
        v_n2 = emit("vector", lambda e: e.tensor_scalar(out=sb["r_nmr"][:], in0=sb["r_nmr"][:],
                                                        scalar1=-1.0, scalar2=None, op0=ALU.mult),
                    waits=[("tV", v_n)], inc="tV")
        t_b1 = emit("tensor", lambda e: e.matmul(ps["P3"][:], sb["c_ones_r"][:], sb["r_rstd"][:], start=True, stop=True),
                    waits=[("tV", v_n2)], inc="tT")
        vb1 = emit("vector", lambda e: e.tensor_copy(sb["bc_rstd"][:], ps["P3"][:]), waits=[("tT", t_b1)], inc="tV")
        t_b2 = emit("tensor", lambda e: e.matmul(ps["P3"][:], sb["c_ones_r"][:], sb["r_nmr"][:], start=True, stop=True),
                    waits=[("tV", vb1)], inc="tT")
        vb2 = emit("vector", lambda e: e.tensor_copy(sb["bc_nmr"][:], ps["P3"][:]), waits=[("tT", t_b2)], inc="tV")
        vv = None
        for dc_ in range(n_dc):
            v_a = emit("vector", lambda e, _dc=dc_: e.tensor_tensor(
                out=sb["t_n0"][:], in0=get_tile(_dc), in1=sb["bc_rstd"][:], op=ALU.mult),
                waits=([("tT", s1_t)] if dc_ == 0 else []) + ([("tV", vv)] if vv else []), inc="tV")
            vv = emit("vector", lambda e, _dc=dc_: e.tensor_tensor(
                out=set_tile(_dc), in0=sb["t_n0"][:], in1=sb["bc_nmr"][:], op=ALU.add),
                waits=[("tV", v_a)], inc="tV")
        return CNT["tV"]

    wait_only("tensor", [("tA", x_cast), ("qS", consts_ld)])
    ln1_v = 0
    for sbk in range(NB):
        ln1_v = ln_block(
            get_tile=lambda dc_, _s=sbk: xnr_sl(dc_, _s * 512, 512),
            set_tile=lambda dc_, _s=sbk: xnr_sl(dc_, _s * 512, 512),
            n_dc=DC, eps_idx=0, sq_wait=[("tA", x_cast)])
    if "dbg_xnr" in dbg_out:
        emit("sync", lambda e: e.dma_start(out=dbg_out["dbg_xnr"], in_=XNR[:]),
             waits=[("tV", ln1_v)], inc="qS", k=16)

    if phase < 2:
        esX.close()
        esQ.close()
        esA.close()
        finish(nc, es, script, SEM, CNT)
        return ap_in

    esX.close()
    esB = contextlib.ExitStack()
    n_coll_z = 0
    _sp = {"i": 0, "tr": []}
    def quant_splice(last=False):
        if phase < 3 or not quant_tasks:
            return
        for tr in _sp["tr"]:
            tr()
        _sp["tr"] = []
        if _sp["i"] == 9:
            emit_gather(1)
        if last:
            emit_gather(2)
            return
        i0 = _sp["i"] * 2
        for t in quant_tasks[i0:i0 + 2]:
            _sp["tr"].append(quant_block(*t))
        _sp["i"] += 1
    if phase >= 3:
        for nm, shp, dt in [("kt", [E, S], BF16), ("qt", [E, S], BF16),
                            ("v2", [128, NKT * 256], BF16),
                            ("wvb", [128, DC * 256], BF16), ("wstg", [128, D // 2], F32),
                            ("pt0", [128, 512], BF16), ("pt1", [128, 512], BF16),
                            ("mxcols", [128, 20], F32),
                            ("mxb", [128, 1], BF16), ("crow", [1, 1], BF16),
                            ("ccol", [128, 1], F32), ("c_bv4", [1, 256], BF16),
                            ("lacc", [128, 512], F32)]:
            sb[nm] = esB.enter_context(nc.sbuf_tensor(nm, shp, dt))
        sb["wkb"] = None
        for pair in range(4):
            h0, h1 = 2 * pair, 2 * pair + 1
            bv_ld = emit("sync", lambda e, _p=pair: e.dma_start(
                out=sb["c_bv4"][:], in_=bvrow[0:1, _p * 256:(_p + 1) * 256]),
                waits=[("tT", CNT["tT"])], inc="bvl", k=16)
            wv_cast = None
            for qtr in range(4):
                ldn = emit("sync", lambda e, _p=pair, _q=qtr: e.dma_start(
                    out=sb["wstg"].ap().rearrange("p (c e) -> p c e", e=256),
                    in_=wv8t[:, _p * 256:(_p + 1) * 256]
                        .rearrange("(c p) e -> p c e", p=128)[:, _q * 4:(_q + 1) * 4, :]),
                    waits=[("tA", CNT["tA"]), ("tT", CNT["tT"])], inc="wst", k=16)
                wv_cast = emit("scalar", lambda e, _q=qtr: e.activation(
                    out=sb["wvb"][:, _q * 1024:(_q + 1) * 1024], in_=sb["wstg"][:],
                    func=AF.Identity, bias=0.0, scale=1.0),
                    waits=[("wst", ldn)], inc="tA")
            v2_done_v = None
            v2_evacs = []
            for kt_ in range(NKT):
                pb = ps["P0"] if kt_ % 2 == 0 else ps["P1"]
                for dc_ in range(DC):
                    wt = ([("tA", wv_cast), ("tV", CNT["tV"])] if (kt_ == 0 and dc_ == 0) else
                          ([("tV", v2_evacs[kt_ - 2])] if dc_ == 0 and kt_ >= 2 else []))
                    emit("tensor", lambda e, _kt=kt_, _dc=dc_, _pb=pb: e.matmul(
                        _pb[:, 0:256], xnr_sl(_dc, _kt * 128, 128), sb["wvb"][:, _dc * 256:(_dc + 1) * 256],
                        start=(_dc == 0), stop=False),
                        waits=wt, inc=None)
                t = emit("tensor", lambda e, _pb=pb: e.matmul(
                    _pb[:, 0:256], sb["c_ones_rb"][:], sb["c_bv4"][:],
                    start=False, stop=True),
                    waits=[("bvl", bv_ld)] if kt_ == 0 else [], inc="tT")
                v2_done_v = emit("vector", lambda e, _kt=kt_, _pb=pb: e.tensor_copy(
                    sb["v2"][:, _kt * 256:(_kt + 1) * 256], _pb[:, 0:256]),
                    waits=[("tT", t)], inc="tV")
                v2_evacs.append(v2_done_v)
            v2_prod_end = CNT["tT"]
            if "dbg_v" in dbg_out and pair == 0:
                emit("sync", lambda e: e.dma_start(out=dbg_out["dbg_v"], in_=sb["v2"][:]),
                     waits=[("tV", v2_done_v)], inc="qS", k=16)

            def emit_wchain(hh_, wait_tT):
                casts = {}
                first = True
                for wsrc, woff in ((wk8, 0), (wq8, 2048)):
                    for half in range(2):
                        wt = [("tA", CNT["tA"])]
                        if first:
                            wt.append(("tT", wait_tT))
                            first = False
                        ldn = emit("sync", lambda e, _h=hh_, _w=wsrc, _f=half: e.dma_start(
                            out=sb["wstg"].ap().rearrange("p (c e) -> p c e", e=128),
                            in_=_w[_h].rearrange("(c p) e -> p c e", p=128)[:, _f * 8:(_f + 1) * 8, :]),
                            waits=wt, inc="wst", k=16)
                        casts[woff] = emit("scalar", lambda e, _o=woff, _f=half: e.activation(
                            out=sb["wvb"][:, _o + _f * 1024: _o + (_f + 1) * 1024], in_=sb["wstg"][:],
                            func=AF.Identity, bias=0.0, scale=1.0),
                            waits=[("wst", ldn)], inc="tA")
                return casts[0], casts[2048]

            wk_cast, wq_cast = emit_wchain(h0, v2_prod_end)
            next_casts = None
            for hh in (h0, h1):
                if hh == h1:
                    wk_cast, wq_cast = next_casts
                kq_done_a = None
                for which, woff2, dst, brow, wld in (("k", 0, "kt", "c_bkt", wk_cast),
                                                     ("q", 2048, "qt", "c_bqt", wq_cast)):
                    kq_evacs = []
                    for sb_ in range(NB):
                        pb = ps["P0"] if sb_ % 2 == 0 else ps["P1"]
                        for dc_ in range(DC):
                            wt = ([("tA", max(wld, CNT["tA"])), ("tV", CNT["tV"])] if (sb_ == 0 and dc_ == 0) else
                                  ([("tA", kq_evacs[sb_ - 2])] if dc_ == 0 and sb_ >= 2 else []))
                            emit("tensor", lambda e, _sb=sb_, _dc=dc_, _pb=pb, _o=woff2: e.matmul(
                                _pb[:], sb["wvb"][:, _o + _dc * 128: _o + (_dc + 1) * 128],
                                xnr_sl(_dc, _sb * 512, 512),
                                start=(_dc == 0), stop=(_dc == DC - 1)),
                                waits=wt, inc="tT" if dc_ == DC - 1 else None)
                        kq_done_a = emit("scalar", lambda e, _sb=sb_, _pb=pb, _d=dst, _b=brow, _h=hh: e.activation(
                            out=sb[_d][:, _sb * 512:(_sb + 1) * 512], in_=_pb[:],
                            func=AF.Identity, bias=sb[_b][:, _h:_h + 1], scale=1.0),
                            waits=[("tT", CNT["tT"])], inc="tA")
                        kq_evacs.append(kq_done_a)
                kqprod_end = CNT["tT"]
                if hh == h0:
                    next_casts = emit_wchain(h1, kqprod_end)
                if "dbg_kt" in dbg_out and hh == 0:
                    emit("sync", lambda e: e.dma_start(out=dbg_out["dbg_kt"], in_=sb["kt"][:]),
                         waits=[("tA", kq_done_a)], inc="qS", k=16)
                    emit("sync", lambda e: e.dma_start(out=dbg_out["dbg_qt"], in_=sb["qt"][:]),
                         waits=[("tA", kq_done_a)], inc="qS", k=16)

                off = (hh - h0) * 128
                emit("vector", lambda e: e.memset(sb["mxcols"][:], -1e30), inc="tV")
                for m in range(NB):
                    if m == 0:
                        p1i = 0
                        vlast = None
                        p1_reds = []
                        for j in range(4):
                            t_ = j
                            pb = [ps["P0"], ps["P1"], ps["P2"], ps["P3"]][p1i % 4]
                            wt = ([("tA", CNT["tA"]), ("tV", CNT["tV"])] if p1i == 0 else
                                  ([("tV", p1_reds[p1i - 4])] if p1i >= 4 else []))
                            tmm = emit("tensor", lambda e, _t=t_, _pb=pb: e.matmul(
                                _pb[:], sb["qt"][:, _t * 128:(_t + 1) * 128],
                                sb["kt"][:, 0:512], start=True, stop=True),
                                waits=wt, inc="tT")
                            vlast = emit("vector", lambda e, _pb=pb, _co=j: e.reduce_max(
                                out=sb["mxcols"][:, _co:_co + 1], in_=_pb[:, 0:512:8], axis=AX.X),
                                waits=[("tT", tmm)], inc="tV")
                            p1_reds.append(vlast)
                            p1i += 1
                        vv = emit("vector", lambda e: e.reduce_max(out=sb["mxb"][:], in_=sb["mxcols"][:, 0:4], axis=AX.X),
                                  waits=[("tV", vlast), ("tT", CNT["tT"])], inc="tV")
                        tt = emit("tensor", lambda e: e.transpose(ps["PTB"][0:1, :], sb["mxb"][:], sb["c_ident"][:]),
                                  waits=[("tV", vv)], inc="tT")
                        vv = emit("vector", lambda e: e.reduce_max(out=sb["crow"][:], in_=ps["PTB"][0:1, :], axis=AX.X),
                                  waits=[("tT", tt)], inc="tV")
                        tt = emit("tensor", lambda e: e.matmul(ps["P2"][:, 0:1], sb["c_neginv"][:], sb["crow"][:],
                                                               start=True, stop=True),
                                  waits=[("tV", vv)], inc="tT")
                        vcc = emit("vector", lambda e: e.tensor_copy(sb["ccol"][:], ps["P2"][:, 0:1]),
                                   waits=[("tT", tt)], inc="tV")
                    nl = 4 * m + 4
                    exp_l = []
                    lacc_v = []

                    def emit_st(l):
                        pst = [ps["P0"], ps["P1"], ps["P2"], ps["P3"]][l % 4]
                        ptb = "pt0" if l % 2 == 0 else "pt1"
                        wt = ([("tV", vcc), ("zst", CNT["zst"])] if l == 0 else
                              ([("tA", exp_l[l - 4])] if l >= 4 else []))
                        tmm = emit("tensor", lambda e, _l=l, _m=m, _pst=pst: e.matmul(
                            _pst[:], sb["kt"][:, _l * 128:(_l + 1) * 128],
                            sb["qt"][:, 4 * _m * 128: 4 * _m * 128 + 512], start=True, stop=True),
                            waits=wt, inc="tT")
                        if l >= 4 * m:
                            r = l - 4 * m
                            w0 = [("tT", tmm)] + ([("tA", exp_l[l - 1])] if l - 1 >= 4 * m else [])
                            vv2 = None
                            for jj in range(4):
                                wj = list(w0) if jj == 0 else [("tV", vv2)]
                                if r == jj:
                                    vv2 = emit("vector", lambda e, _pst=pst, _j=jj: e.tensor_tensor(
                                        out=sb["t_n0"][:, _j * 128:(_j + 1) * 128],
                                        in0=_pst[:, _j * 128:(_j + 1) * 128],
                                        in1=sb["c_masks"][:, 0:128], op=ALU.add),
                                        waits=wj, inc="tV")
                                elif r > jj:
                                    vv2 = emit("vector", lambda e, _pst=pst, _j=jj: e.tensor_tensor(
                                        out=sb["t_n0"][:, _j * 128:(_j + 1) * 128],
                                        in0=_pst[:, _j * 128:(_j + 1) * 128],
                                        in1=sb["c_masks"][:, 128:256], op=ALU.add),
                                        waits=wj, inc="tV")
                                else:
                                    vv2 = emit("vector", lambda e, _pst=pst, _j=jj: e.tensor_copy(
                                        sb["t_n0"][:, _j * 128:(_j + 1) * 128],
                                        _pst[:, _j * 128:(_j + 1) * 128]),
                                        waits=wj, inc="tV")
                            aa = emit("scalar", lambda e, _pt=ptb: e.activation(
                                out=sb[_pt][:], in_=sb["t_n0"][:], func=AF.Exp,
                                bias=sb["ccol"][:], scale=INV),
                                waits=[("tV", vv2), ("tT", CNT["tT"])] +
                                      ([("tV", lacc_v[l - 2])] if l >= 2 and len(lacc_v) > l - 2 else []),
                                inc="tA")
                        else:
                            aa = emit("scalar", lambda e, _pst=pst, _pt=ptb: e.activation(
                                out=sb[_pt][:], in_=_pst[:], func=AF.Exp,
                                bias=sb["ccol"][:], scale=INV),
                                waits=[("tT", tmm)] +
                                      ([("tV", lacc_v[l - 2])] if l >= 2 and len(lacc_v) > l - 2 else []),
                                inc="tA")
                        exp_l.append(aa)

                    def emit_z(l):
                        ptb = "pt0" if l % 2 == 0 else "pt1"
                        wtz = [("tA", exp_l[l])] + ([("tV", CNT["tV"])] if l == 0 else [])
                        emit("tensor", lambda e, _l=l, _pt=ptb, _off=off, _nl=nl: e.matmul(
                            ps["PZ"][:], sb["v2"][:, _l * 256 + _off: _l * 256 + _off + 128],
                            sb[_pt][:], start=(_l == 0), stop=(_l == _nl - 1)),
                            waits=wtz, inc="tT")
                        if LACC:
                            if l == 0:
                                lacc_v.append(emit("vector", lambda e, _pt=ptb: e.tensor_copy(sb["lacc"][:], sb[_pt][:]),
                                     waits=[("tA", exp_l[l]), ("tT", CNT["tT"])], inc="tV"))
                            else:
                                lacc_v.append(emit("vector", lambda e, _pt=ptb: e.tensor_tensor(
                                    out=sb["lacc"][:], in0=sb["lacc"][:], in1=sb[_pt][:], op=ALU.add),
                                    waits=[("tA", exp_l[l])], inc="tV"))
                        else:
                            lacc_v.append(emit("tensor", lambda e, _l=l, _pt=ptb, _nl=nl: e.matmul(
                                ps["PL"][0:1, :], sb["c_ones_cb"][:], sb[_pt][:],
                                start=(_l == 0), stop=(_l == _nl - 1)), inc="tT") * 0 + CNT["tV"])

                    if PIPE_Z:
                        for l in range(nl):
                            emit_st(l)
                            if l >= 1:
                                emit_z(l - 1)
                        emit_z(nl - 1)
                    else:
                        for l in range(nl):
                            emit_st(l)
                            emit_z(l)
                    vl_done = CNT["tV"]
                    if LACC:
                        tt = emit("tensor", lambda e: e.matmul(ps["PL"][0:1, :], sb["c_ones_cf"][:], sb["lacc"][:],
                                                               start=True, stop=True),
                                  waits=[("tV", vl_done)], inc="tT")
                    else:
                        tt = CNT["tT"]
                    vv = emit("vector", lambda e: e.reciprocal(out=sb["r_s1"][:], in_=ps["PL"][0:1, :]),
                              waits=[("tT", tt)], inc="tV")
                    tt = emit("tensor", lambda e: e.matmul(ps["P0"][:], sb["c_ones_r"][:], sb["r_s1"][:],
                                                           start=True, stop=True),
                              waits=[("tV", vv)], inc="tT")
                    vv = emit("vector", lambda e: e.tensor_copy(sb["bc_rstd"][:], ps["P0"][:]),
                              waits=[("tT", tt)], inc="tV")
                    vz = emit("vector", lambda e: e.tensor_tensor(out=sb["pt0"][:], in0=ps["PZ"][:],
                                                                  in1=sb["bc_rstd"][:], op=ALU.mult),
                              waits=[("zst", CNT["zst"]), ("tV", vv)], inc="tV")
                    hf, mm_ = m // 4, m % 4
                    emit("sync", lambda e, _hf=hf, _m=mm_, _p=pair, _o=off: e.dma_start(
                        out=zin.ap()[_hf * 1024 + _p * 256 + _o: _hf * 1024 + _p * 256 + _o + 128,
                                     _m * 512:(_m + 1) * 512],
                        in_=sb["pt0"][:]), waits=[("tV", vz)], inc="zst", k=16)
                    if hh == h1 and m == 3:
                        zw0 = CNT["zst"]
                        emit("gpsimd", lambda e, _p=pair: e.collective_compute(
                            "AllGather", ALU.bypass, replica_groups=[[0, 1], [2, 3], [4, 5], [6, 7]],
                            ins=[zin.ap()[_p * 256: (_p + 1) * 256, :].opt()],
                            outs=[ZG.ap()[_p * 512: (_p + 1) * 512, :].opt()]),
                            waits=[("zst", zw0)], inc="cc")
                    if m == 3:
                        quant_splice()
                quant_splice()
            zw = CNT["zst"]
            emit("gpsimd", lambda e, _p=pair: e.collective_compute(
                "AllGather", ALU.bypass, replica_groups=[[0, 1], [2, 3], [4, 5], [6, 7]],
                ins=[zin.ap()[1024 + _p * 256: 1024 + (_p + 1) * 256, :].opt()],
                outs=[ZG.ap()[2048 + _p * 512: 2048 + (_p + 1) * 512, :].opt()]),
                waits=[("zst", zw)], inc="cc")
            n_coll_z += 2
            zw = CNT["zst"]
            emit("gpsimd", lambda e, _p=pair: e.collective_compute(
                "AllGather", ALU.bypass, replica_groups=[[0, 1], [2, 3], [4, 5], [6, 7]],
                ins=[zin.ap()[1024 + _p * 256: 1024 + (_p + 1) * 256, :].opt()],
                outs=[ZG.ap()[2048 + _p * 512: 2048 + (_p + 1) * 512, :].opt()]),
                waits=[("zst", zw)], inc="cc")
            n_coll_z += 2
        quant_splice(last=True)
        if "dbg_z" in dbg_out:
            emit("sync", lambda e: e.dma_start(out=dbg_out["dbg_z"], in_=zin.ap()),
                 waits=[("qS", CNT["qS"])], inc="qS", k=16)
    cc_z = CNT["cc"]

    if phase < 4:
        esB.close()
        esQ.close()
        esA.close()
        finish(nc, es, script, SEM, CNT)
        return ap_in

    esB.close()
    esQ.close()
    esA.close()
    esC = contextlib.ExitStack()
    sb["xn2"] = esC.enter_context(nc.sbuf_tensor("xn2", [128, DC * OWNQ], BF16))
    esD = contextlib.ExitStack()
    for nm, shp, dt in [("wob", [128, H * D], BF16), ("wostg0", [128, D], F32),
                        ("wostg1", [128, D], F32),
                        ("zt", [128, H * 512], BF16),
                        ("xqt_t", [128, 512], F32), ("x1b", [128, DC * 512], BF16),
                        ("x1f", [128, 512], F32)]:
        sb[nm] = esD.enter_context(nc.sbuf_tensor(nm, shp, dt))

    wo_cast = None
    wo_casts = []
    for h in range(H):
        stg = "wostg0" if h % 2 == 0 else "wostg1"
        wsm = "wst" if h % 2 == 0 else "w1a"
        wt = ([("tT", CNT["tT"]), ("tV", CNT["tV"]), ("tA", CNT["tA"]), ("qS", CNT["qS"])]
              if h == 0 else ([("tA", wo_casts[h - 2])] if h >= 2 else []))
        ldn = emit("sync", lambda e, _h=h, _s=stg: e.dma_start(
            out=sb[_s][:], in_=wo[_h]), waits=wt, inc=wsm, k=16)
        wo_cast = emit("scalar", lambda e, _h=h, _s=stg: e.activation(
            out=sb["wob"][:, _h * D:(_h + 1) * D], in_=sb[_s][:],
            func=AF.Identity, bias=0.0, scale=1.0), waits=[(wsm, ldn)], inc="tA")
        wo_casts.append(wo_cast)

    ln2_done = 0
    x1st_prev = 0
    for qb in range(4):
        def zread_qb(e, _qb=qb):
            with e.register("rz%d" % _qb) as rz:
                e.reg_load(rz, sb["s_zoff"][0:1, 0:1])
                v = e.snap(rz, min_val=0, max_val=2048)
                src_ap = ZG.ap()[bass.ds(v, 2048), _qb * 512:(_qb + 1) * 512]
                src_ap = src_ap.rearrange("(b p) q -> p b q", p=128)
                last = e.dma_start(out=sb["zt"].ap().rearrange("p (b q) -> p b q", q=512),
                                   in_=src_ap)
                last.then_inc(SEM["ztl"], 16)
            return last
        wt = [("cc", cc_z), ("qS", consts_ld)] if qb == 0 else [("tT", CNT["tT"])]
        emit_multi_dma("sync", zread_qb, waits=wt, n_dma=1, sem="ztl")
        zld = CNT["ztl"]
        for dt in range(DC):
            pb = ps["P0"] if dt % 2 == 0 else ps["P1"]
            xq_ld = emit("gpsimd", lambda e, _dt=dt, _qb=qb: e.dma_start(
                out=sb["xqt_t"][:], in_=xqT[_dt * 128:(_dt + 1) * 128, _qb * 512:(_qb + 1) * 512]),
                waits=[("tV", CNT["tV"])], inc="xq", k=16)
            for h in range(H):
                zsl = ((h % 8) // 2) * 4 + (h // 8) * 2 + (h % 2)
                emit("tensor", lambda e, _h=h, _z=zsl, _dt=dt, _pb=pb: e.matmul(
                    _pb[:], sb["wob"][:, _h * D + _dt * 128: _h * D + _dt * 128 + 128],
                    sb["zt"][:, _z * 512:(_z + 1) * 512],
                    start=(_h == 0), stop=(_h == H - 1)),
                    waits=([("tA", wo_cast), ("ztl", zld), ("tV", CNT["tV"])] if (dt == 0 and h == 0) else
                           ([("tV", CNT["tV"])] if h == 0 and dt >= 2 else [])),
                    inc="tT" if h == H - 1 else None)
            vv = emit("vector", lambda e, _dt=dt, _pb=pb: e.scalar_tensor_tensor(
                out=sb["x1f"][:], in0=_pb[:], scalar=sb["c_bot"][:, _dt:_dt + 1],
                in1=sb["xqt_t"][:], op0=ALU.add, op1=ALU.add),
                waits=[("tT", CNT["tT"]), ("xq", xq_ld)] +
                      ([("qS", x1st_prev)] if x1st_prev else []), inc="tV")
            emit("vector", lambda e, _dt=dt: e.tensor_copy(sb["x1b"][:, _dt * 512:(_dt + 1) * 512], sb["x1f"][:]),
                 waits=[("tV", vv)], inc="tV")
            st = emit("sync", lambda e, _dt=dt, _qb=qb: e.dma_start(
                out=x1s.ap()[_dt * 128:(_dt + 1) * 128, _qb * 512:(_qb + 1) * 512], in_=sb["x1f"][:]),
                waits=[("tV", CNT["tV"])], inc="qS", k=16)
            x1st_prev = st
        ln2_done = ln_block(
            get_tile=lambda dc_: sb["x1b"][:, dc_ * 512:(dc_ + 1) * 512],
            set_tile=lambda dc_, _qb=qb: sb["xn2"][:, dc_ * OWNQ + _qb * 512: dc_ * OWNQ + _qb * 512 + 512],
            n_dc=DC, eps_idx=0)
    if "dbg_x1" in dbg_out:
        emit("sync", lambda e: e.dma_start(out=dbg_out["dbg_x1"], in_=x1s.ap()),
             waits=[("qS", CNT["qS"]), ("tV", CNT["tV"])], inc="qS", k=16)
    if "dbg_xn2" in dbg_out:
        emit("sync", lambda e: e.dma_start(out=dbg_out["dbg_xn2"], in_=sb["xn2"][:]),
             waits=[("tV", ln2_done)], inc="qS", k=16)

    if phase < 5:
        esD.close()
        esC.close()
        finish(nc, es, script, SEM, CNT)
        return ap_in

    esD.close()
    esE = contextlib.ExitStack()
    for nm, shp, dt in [("w1sb0", [128, D], BF16), ("w1sb1", [128, D], BF16),
                        ("a_st", [128, 64 * 512], BF16), ("g_t", [128, 512], F32),
                        ("a2_t", [128, 512], BF16), ("w2dt0", [128, 64 * 128], BF16),
                        ("w2dt1", [128, 64 * 128], BF16), ("x1r", [128, 512], F32),
                        ("yt_t", [128, 512], F32)]:
        sb[nm] = esE.enter_context(nc.sbuf_tensor(nm, shp, dt))

    y_st = 0
    for sbk in range(4):
        aV = 0
        pa_mm = []
        pa_relu = []
        for c in range(64):
            wbuf = "w1sb0" if c % 2 == 0 else "w1sb1"
            rank, loc = c // 8, c % 8
            wsem = "w1a" if c % 2 == 0 else "w1b"
            wq = "gpsimd" if c % 2 == 0 else "sync"
            wld = emit(wq, lambda e, _r=rank, _l=loc, _w=wbuf: e.dma_start(
                out=sb[_w][:],
                in_=w1tg.ap()[_r * 1024 + _l * 128: _r * 1024 + (_l + 1) * 128, :]),
                waits=([("ccw", cc_w), ("tT", CNT["tT"]), ("tV", CNT["tV"]), ("tA", CNT["tA"])]
                       if (sbk == 0 and c == 0) else
                       ([("tT", pa_mm[c - 2])] if c >= 2 else [("tT", CNT["tT"])])),
                inc=wsem, k=16)
            pb = ps["P0"] if c % 2 == 0 else ps["P1"]
            for dc_ in range(DC):
                emit("tensor", lambda e, _c=c, _dc=dc_, _pb=pb, _w=wbuf, _s=sbk: e.matmul(
                    _pb[:], sb[_w][:, _dc * 128:(_dc + 1) * 128],
                    sb["xn2"][:, _dc * OWNQ + _s * 512: _dc * OWNQ + _s * 512 + 512],
                    start=(_dc == 0), stop=(_dc == DC - 1)),
                    waits=([(wsem, wld), ("tV", ln2_done)] if dc_ == 0 else []) +
                          ([("tA", pa_relu[c - 2])] if (dc_ == 0 and c >= 2) else []),
                    inc="tT" if dc_ == DC - 1 else None)
            pa_mm.append(CNT["tT"])
            aa = emit("scalar", lambda e, _pb=pb: e.activation(
                out=sb["g_t"][:], in_=_pb[:], func=AF.Relu, bias=0.0, scale=1.0),
                waits=[("tT", CNT["tT"]), ("tV", CNT["tV"])], inc="tA")
            pa_relu.append(aa)
            va = emit("vector", lambda e, _c=c: e.tensor_tensor(
                out=sb["a_st"][:, _c * 512:(_c + 1) * 512], in0=sb["g_t"][:], in1=sb["g_t"][:], op=ALU.mult),
                waits=[("tA", aa), ("tT", CNT["tT"])], inc="tV")
            v2_ = emit("vector", lambda e, _c=c: e.tensor_tensor(
                out=sb["a2_t"][:], in0=sb["a_st"][:, _c * 512:(_c + 1) * 512],
                in1=sb["a_st"][:, _c * 512:(_c + 1) * 512], op=ALU.mult),
                waits=[("tT", CNT["tT"]), ("tV", va)], inc="tV")
            emit("tensor", lambda e, _c=c: e.matmul(
                ps["PL"][0:1, :], sb["c_ones_cb"][:], sb["a2_t"][:],
                start=(_c == 0), stop=(_c == 63)),
                waits=[("tV", v2_)], inc="tT")
            aV = CNT["tV"]
        vv = emit("vector", lambda e: e.tensor_scalar(out=sb["r_ms"][:], in0=ps["PL"][0:1, :],
                                                      scalar1=1.0 / HID, scalar2=None, op0=ALU.mult),
                  waits=[("tT", CNT["tT"])], inc="tV")
        a1 = emit("scalar", lambda e: e.activation(out=sb["r_nmr"][:], in_=sb["r_ms"][:], func=AF.Ln,
                                                   bias=sb["c_eps"][0:1, 1:2], scale=1.0),
                  waits=[("tV", vv)], inc="tA")
        a2_ = emit("scalar", lambda e: e.activation(out=sb["r_rstd"][:], in_=sb["r_nmr"][:], func=AF.Exp, bias=0.0, scale=-0.5),
                   waits=[("tA", a1)], inc="tA")
        tt = emit("tensor", lambda e: e.matmul(ps["P2"][:], sb["c_ones_r"][:], sb["r_rstd"][:], start=True, stop=True),
                  waits=[("tA", a2_)], inc="tT")
        vrr = emit("vector", lambda e: e.tensor_copy(sb["bc_rstd"][:], ps["P2"][:]), waits=[("tT", tt)], inc="tV")
        if "dbg_a" in dbg_out and sbk == 0:
            emit("sync", lambda e: e.dma_start(out=dbg_out["dbg_a"], in_=sb["a_st"][:]),
                 waits=[("tV", aV)], inc="qS", k=16)
        if "dbg_bc" in dbg_out and sbk == 0:
            emit("sync", lambda e: e.dma_start(out=dbg_out["dbg_bc"], in_=sb["bc_rstd"][:]),
                 waits=[("tV", vrr)], inc="qS", k=16)
        pb_mm = []
        for dt in range(DC):
            y_prev = y_st
            wbuf = "w2dt0" if dt % 2 == 0 else "w2dt1"
            rank, half = dt // 2, dt % 2
            w2sem = "w2a" if dt % 2 == 0 else "w2b"
            w2q = "gpsimd" if dt % 2 == 0 else "sync"
            w2ld = emit(w2q, lambda e, _r=rank, _h=half, _w=wbuf: e.dma_start(
                out=sb[_w][:],
                in_=w2tg.ap()[_r * 256 + _h * 128: _r * 256 + (_h + 1) * 128, :]),
                waits=[("tT", pb_mm[dt - 2])] if dt >= 2 else [("tT", CNT["tT"])], inc=w2sem, k=16)
            x1ld = emit("sync", lambda e, _dt=dt, _s=sbk: e.dma_start(
                out=sb["x1r"][:], in_=x1s.ap()[_dt * 128:(_dt + 1) * 128, _s * 512:(_s + 1) * 512]),
                waits=[("tV", CNT["tV"])], inc="x1l", k=16)
            pb = ps["P2"] if dt % 2 == 0 else ps["P3"]
            for c in range(64):
                emit("tensor", lambda e, _c=c, _dt=dt, _pb=pb, _w=wbuf: e.matmul(
                    _pb[:], sb[_w][:, _c * 128:(_c + 1) * 128],
                    sb["a_st"][:, _c * 512:(_c + 1) * 512],
                    start=(_c == 0), stop=(_c == 63)),
                    waits=([(w2sem, w2ld), ("tV", vrr)] if c == 0 else []) +
                          ([("tV", CNT["tV"])] if (c == 0 and dt >= 2) else []),
                    inc="tT" if c == 63 else None)
            pb_mm.append(CNT["tT"])
            vv = emit("vector", lambda e, _pb=pb: e.tensor_tensor(
                out=sb["yt_t"][:], in0=_pb[:], in1=sb["bc_rstd"][:], op=ALU.mult),
                waits=[("tT", CNT["tT"]), ("x1l", x1ld)] +
                      ([("qS", y_prev)] if y_prev else []), inc="tV")
            vv = emit("vector", lambda e: e.tensor_tensor(
                out=sb["yt_t"][:], in0=sb["yt_t"][:], in1=sb["x1r"][:], op=ALU.add),
                waits=[("tV", vv)], inc="tV")
            y_st = emit("sync", lambda e, _dt=dt, _s=sbk: e.dma_start(
                out=yT[_dt * 128:(_dt + 1) * 128, _s * 512:(_s + 1) * 512], in_=sb["yt_t"][:]),
                waits=[("tV", vv)], inc="qS", k=16)

    if "dbg_w1g2" in dbg_out:
        emit("sync", lambda e: e.dma_start(out=dbg_out["dbg_w1g2"], in_=w1tg.ap()),
             waits=[("tT", CNT["tT"]), ("tV", CNT["tV"]), ("tA", CNT["tA"])], inc="qS", k=16)
    esE.close()
    esC.close()
    finish(nc, es, script, SEM, CNT)
    return ap_in


def finish(nc, es, script, SEM, CNT):
    with nc.Block() as block:
        @block.sync
        def _(sync):
            for eng, f in script:
                if eng == "sync":
                    f(sync)
            for s in ("qS", "w1a", "w1b", "w2a", "w2b", "x1l", "xq", "ztl", "wst", "xsa", "xsb", "qld", "zst", "bvl", "tG" if False else "bvl"):
                if CNT[s]:
                    sync.wait_ge(SEM[s], CNT[s])

        @block.tensor
        def _(tensor):
            for eng, f in script:
                if eng == "tensor":
                    f(tensor)

        @block.vector
        def _(vector):
            for eng, f in script:
                if eng == "vector":
                    f(vector)

        @block.scalar
        def _(scalar):
            for eng, f in script:
                if eng == "scalar":
                    f(scalar)

        @block.gpsimd
        def _(gpsimd):
            for eng, f in script:
                if eng == "gpsimd":
                    f(gpsimd)
            if CNT["qG"]:
                gpsimd.wait_ge(SEM["qG"], CNT["qG"])
    es.close()


def make_masks():
    kk = np.arange(128)[:, None]
    qq = np.arange(128)[None, :]
    tri = np.where(kk > qq, np.float32(NEG_BIG), np.float32(0.0))
    neg = np.full((128, 128), NEG_BIG, np.float32)
    return np.concatenate([tri, neg], axis=1).astype(ml_dtypes.bfloat16)


def make_inputs(core, x, W_Q, W_K, W_V, W_O, b_Q, b_K, b_V, b_O, W_fc, W_proj):
    b, p = core // 2, core % 2
    own = slice(p * 8, p * 8 + 8)
    d = {}
    d["x_bT"] = np.ascontiguousarray(x[b].T)
    d["xqT"] = np.ascontiguousarray(x[b, p * OWNQ:(p + 1) * OWNQ].T)
    d["wq8"] = np.ascontiguousarray(W_Q[own])
    d["wk8"] = np.ascontiguousarray(W_K[own])
    d["wv8t"] = np.ascontiguousarray(np.transpose(W_V[own], (1, 0, 2)).reshape(D, 8 * E))
    d["bqt"] = np.ascontiguousarray(b_Q[own].T)
    d["bkt"] = np.ascontiguousarray(b_K[own].T)
    d["bvrow"] = np.ascontiguousarray(b_V[own].reshape(1, 8 * E)).astype(ml_dtypes.bfloat16)
    d["wo"] = np.ascontiguousarray(W_O)
    d["bot"] = np.ascontiguousarray(b_O.reshape(DC, 128).T)
    d["wfc_s"] = np.ascontiguousarray(W_fc[core * (HID // 8):(core + 1) * (HID // 8)])
    d["wpj_s"] = np.ascontiguousarray(W_proj[core * (D // 8):(core + 1) * (D // 8)])
    d["zoffv"] = np.array([[p * 2048]], np.int32)
    d["identb"] = np.eye(128, dtype=ml_dtypes.bfloat16)
    d["onesrow_f"] = np.ones((1, 128), np.float32)
    d["onesrow_b"] = np.ones((1, 128), ml_dtypes.bfloat16)
    d["onescol_f"] = np.ones((128, 1), np.float32)
    d["onescol_b"] = np.ones((128, 1), ml_dtypes.bfloat16)
    d["neginvrow"] = np.full((1, 128), -INV, ml_dtypes.bfloat16)
    d["masksb"] = make_masks()
    d["epsrow"] = np.array([[EPS1, EPS2]], np.float32)
    return d


def kernel(L0_attn_x, L0_attn_W_Q, L0_attn_W_K, L0_attn_W_V, L0_attn_W_O,
           L0_attn_b_Q, L0_attn_b_K, L0_attn_b_V, L0_attn_b_O,
           L0_ffn_W_fc, L0_ffn_W_proj):
    args = [np.asarray(a) for a in (L0_attn_x, L0_attn_W_Q, L0_attn_W_K, L0_attn_W_V,
                                    L0_attn_W_O, L0_attn_b_Q, L0_attn_b_K, L0_attn_b_V,
                                    L0_attn_b_O, L0_ffn_W_fc, L0_ffn_W_proj)]
    nc = bass.Bass("TRN2", target_bir_lowering=False)
    build(nc, phase=FULL_PHASE)
    in_maps = [make_inputs(c, *args) for c in range(8)]
    res = run_bass_kernel_spmd(nc, in_maps, core_ids=list(range(8)))
    out = np.empty((B, S, D), np.float32)
    for c in range(8):
        b, p = c // 2, c % 2
        out[b, p * OWNQ:(p + 1) * OWNQ, :] = res.results[c]["yT"].T
    return out
